# version 06 (final)
# baseline (speedup 1.0000x reference)
import math
import numpy as np
import ml_dtypes
import concourse.bass as bass
import concourse.mybir as mybir
from concourse.bass_utils import run_bass_kernel_spmd
from concourse.alu_op_type import AluOpType as Op

AF = mybir.ActivationFunctionType
F32 = mybir.dt.float32
BF16 = mybir.dt.bfloat16
F16 = mybir.dt.float16
I32 = mybir.dt.int32
BFnp = ml_dtypes.bfloat16

T = 2048
DM = 2048
HEADS, DK, DV, M = 16, 64, 128, 64
KEY_DIM, VAL_DIM = HEADS * DK, HEADS * DV
FFN_I = 5632
GU_OUT = 2 * FFN_I
DI = 4096
DS = 16
DTR = 128
XDBL = DTR + 2 * DS
ARENA = 52000


class St:
    def __init__(self, name):
        self.name = name
        self.ops = []
        self.n = 0
        self.lastwait = {}

    def raw(self, fn):
        self.ops.append(fn)

    def inc(self, fn, k=1):
        self.n += k
        s = self

        def wrapped(e, f=fn, k=k, s=s):
            f(e).then_inc(s.sem, k)
        self.ops.append(wrapped)
        return (self, self.n)

    def tok(self):
        return (self, self.n)


class Arena:
    def __init__(self, ap, size_i32):
        self.ap = ap
        self.size = size_i32
        self.off = 0
        self.marks = []
        self.hwm = 0

    def mark(self):
        self.marks.append(self.off)

    def release(self):
        self.off = self.marks.pop()

    def alloc(self, cols, dt):
        w = 1 if dt in (F32, I32) else 2
        n = (cols + w - 1) // w
        assert self.off + n <= self.size, (
            f"SBUF arena overflow: need {n} i32 at {self.off}/{self.size}")
        view = self.ap[:, self.off:self.off + n]
        self.off += n
        self.hwm = max(self.hwm, self.off)
        if dt != I32:
            view = view.bitcast(dt)
        return view[:, :cols]


class MK:
    def __init__(self, nc):
        self.nc = nc
        self.pe = St("pe")
        self.dve = St("dve")
        self.act = St("act")
        self.pool = St("pool")
        self.sp = St("sp")
        self.streams = [self.pe, self.dve, self.act, self.pool, self.sp]
        self.qsp = St("qsp")
        self.qpl = St("qpl")
        self.queues = [self.qsp, self.qpl]
        self.final_deps = []

    def wait(self, cons, tok):
        src, val = tok
        if val <= 0 or src is cons:
            return
        if cons.lastwait.get(src.name, -1) >= val:
            return
        cons.lastwait[src.name] = val

        def w(e, s=src, v=val):
            e.wait_ge(s.sem, v)
        cons.ops.append(w)

    def dma(self, issuer, q, out_ap, in_ap, deps=()):
        for t in deps:
            self.wait(issuer, t)
        q.n += 16

        def d(e, o=out_ap, i=in_ap, qq=q):
            e.dma_start(out=o, in_=i).then_inc(qq.sem, 16)
        issuer.ops.append(d)
        return (q, q.n)

    def mm(self, out, lhsT, rhs, start, stop, deps=(), inc=False):
        for t in deps:
            self.wait(self.pe, t)
        fn = (lambda e, o=out, l=lhsT, r=rhs, s1=start, s2=stop:
              e.matmul(o, l, r, start=s1, stop=s2))
        return self.pe.inc(fn) if inc else self.pe.raw(fn)

    def tp(self, out, in_, ident, deps=(), inc=False):
        for t in deps:
            self.wait(self.pe, t)
        fn = lambda e, o=out, i=in_, d=ident: e.transpose(o, i, d)
        return self.pe.inc(fn) if inc else self.pe.raw(fn)

    def ev(self, stream, fn, deps=(), inc=True):
        for t in deps:
            self.wait(stream, t)
        return stream.inc(fn) if inc else stream.raw(fn)

    def phase_barrier(self):
        toks = [(s, s.n) for s in
                (self.pe, self.dve, self.act, self.pool,
                 self.qsp, self.qpl)]
        for cons in (self.sp, self.pool, self.dve, self.act, self.pe):
            for tk in toks:
                if tk[0] is not cons:
                    self.wait(cons, tk)


def build_program(dbg=(), phases="P0ABCD"):
    nc = bass.Bass("TRN2", target_bir_lowering=False,
                   detect_race_conditions=False)
    mk = MK(nc)
    t = {}

    def inp(name, shape, dt):
        t[name] = nc.dram_tensor(name, shape, dt, kind="ExternalInput").ap()

    inp("x0T", [DM, T], F32)
    inp("Wq", [DM, KEY_DIM], F32)
    inp("Wk", [DM, KEY_DIM], F32)
    inp("Wv", [DM, VAL_DIM], F32)
    inp("Wg", [DM, VAL_DIM], F32)
    inp("Ws", [DM, HEADS * M], F32)
    inp("Wo", [VAL_DIM, DM], F32)
    inp("gnw", [DV, 1], F32)
    inp("W_inT", [DM, 2 * DI], F32)
    inp("conv_w", [128, 32 * 4], F32)
    inp("conv_b", [128, 32], F32)
    inp("W_xT", [DI, XDBL], F32)
    inp("W_dtT", [DTR, DI], F32)
    inp("dt_bias", [128, 32], F32)
    inp("A_log", [128, 32 * DS], F32)
    inp("Dvec", [128, 32], F32)
    inp("W_outT", [DI, DM], F32)
    inp("gupT", [DM // 8, GU_OUT], I32)
    inp("gus", [DM // 128, GU_OUT], F16)
    inp("dnpT", [FFN_I // 8, DM], I32)
    inp("dns", [FFN_I // 128, DM], F16)
    inp("Hmat", [DI, DI], BF16)
    inp("triu_bf", [128, 128], BF16)
    inp("ident_bf", [128, 128], BF16)
    inp("ident_f32", [128, 128], F32)
    inp("ones_col_f32", [1, 128], F32)
    inp("ones_col_bf", [1, 128], BF16)
    inp("ones128_f32", [128, 1], F32)
    inp("ones128_bf", [128, 1], BF16)
    inp("constf", [128, 4], F32)
    inp("rep8", [8, 128], F32)

    t["outT"] = nc.dram_tensor("outT", [DM, T], F32,
                               kind="ExternalOutput").ap()

    def scr(name, shape, dt):
        kind = "ExternalOutput" if name in dbg else "Internal"
        t[name] = nc.dram_tensor(name, shape, dt, kind=kind).ap()

    scr("x1T", [DM, T], F32)
    scr("x2T", [DM, T], F32)
    scr("x3T", [DM, T], F32)
    scr("wguS", [DM, GU_OUT], BF16)
    scr("wdnS", [FFN_I, DM], BF16)
    scr("uS", [DI, T], BF16)
    scr("szS", [DI, T], BF16)
    scr("bcS", [2 * DS, T], BF16)
    scr("dtTS", [DTR, T], BF16)
    scr("yS", [DI, T], BF16)
    scr("woutS", [DI, DM], BF16)
    scr("dbgS", [128, 5 * DS * 512 + 512], F32)
    scr("WqS", [DM, KEY_DIM], BF16)
    scr("WkS", [DM, KEY_DIM], BF16)
    scr("WsS", [DM, HEADS * M], BF16)
    scr("WvS", [DM, VAL_DIM], BF16)
    scr("WgS", [DM, VAL_DIM], BF16)
    scr("WoS", [VAL_DIM, DM], BF16)
    scr("WinS", [DM, 2 * DI], BF16)
    scr("WxS", [DI, XDBL], BF16)
    scr("WdtS", [DTR, DI], BF16)
    scr("gusS", [DM // 128, GU_OUT], F32)
    scr("dnsS", [FFN_I // 128, DM], F32)

    import contextlib
    stack = contextlib.ExitStack()
    arena_t = stack.enter_context(nc.sbuf_tensor("arena", [128, ARENA], I32))
    pbt = [stack.enter_context(nc.psum_tensor("pb%d" % i, [128, 512], F32))
           for i in range(8)]
    for s in mk.streams + mk.queues:
        s.sem = stack.enter_context(nc.semaphore("sem_" + s.name))
    with stack:
        ar = Arena(arena_t[:, :], ARENA)
        pb = [x[:, :] for x in pbt]
        _emit_all(mk, t, ar, pb, phases)
        with nc.Block() as block:
            @block.tensor
            def _(e):
                for f in mk.pe.ops:
                    f(e)

            @block.vector
            def _(e):
                for f in mk.dve.ops:
                    f(e)

            @block.scalar
            def _(e):
                for f in mk.act.ops:
                    f(e)

            @block.gpsimd
            def _(e):
                for f in mk.pool.ops:
                    f(e)

            @block.sync
            def _(e):
                for f in mk.sp.ops:
                    f(e)
    return nc


def _emit_all(mk, t, ar, pb, phases="P0ABCD"):
    g = {"dbgln": "LN" in phases}
    if "W2S" in phases:
        g["awslots"] = 2
    if "OVS" in phases:
        g["dbg_ovs"] = True
    if "PROJ" in phases:
        g["dbg_proj"] = True
    if "OVG" in phases:
        g["dbg_ovg"] = True
    if "SEQH" in phases:
        g["seqheads"] = True
    if "DUCONST" in phases:
        g["du_const"] = True
    if "BRCONST" in phases:
        g["du_const"] = True
        g["brep_const"] = True
    if "SDELAY" in phases:
        g["scan_delay"] = True
    if "SCONSTAB" in phases:
        g["scan_const"] = "AB"
    elif "SCONSTA" in phases:
        g["scan_const"] = "A"
    elif "SCONSTB" in phases:
        g["scan_const"] = "B"
    if "NOHAD" in phases:
        g["nohad"] = True
    if "DS1" in phases:
        g["dbg_scan"] = (1, 1)
    if "DS0" in phases:
        g["dbg_scan"] = (0, 0)
    if "T1" in phases:
        g["trunc"] = 1
    if "T2" in phases:
        g["trunc"] = 2
    g["gnw"] = ar.alloc(1, F32)
    g["cw"] = ar.alloc(32 * 4, F32)
    g["cb"] = ar.alloc(32, F32)
    g["dtb"] = ar.alloc(32, F32)
    g["Dv"] = ar.alloc(32, F32)
    g["negA"] = ar.alloc(32 * DS, F32)
    g["triu"] = ar.alloc(128, BF16)
    g["id_bf"] = ar.alloc(128, BF16)
    g["id_f32"] = ar.alloc(128, F32)
    g["oc_f32"] = ar.alloc(128, F32)
    g["oc_bf"] = ar.alloc(128, BF16)
    g["o128_f32"] = ar.alloc(1, F32)
    g["o128_bf"] = ar.alloc(1, BF16)
    g["constf"] = ar.alloc(4, F32)
    for col, val in ((0, 1.0), (1, 1e-5), (2, 1e-6)):
        mk.nc.const_aps.register(F32, val, g["constf"][:, col:col + 1])

    dep = []
    dep.append(mk.dma(mk.sp, mk.qsp, g["gnw"], t["gnw"]))
    dep.append(mk.dma(mk.sp, mk.qsp, g["cw"], t["conv_w"]))
    dep.append(mk.dma(mk.sp, mk.qsp, g["cb"], t["conv_b"]))
    dep.append(mk.dma(mk.sp, mk.qsp, g["dtb"], t["dt_bias"]))
    dep.append(mk.dma(mk.sp, mk.qsp, g["Dv"], t["Dvec"]))
    dep.append(mk.dma(mk.sp, mk.qsp, g["triu"], t["triu_bf"]))
    dep.append(mk.dma(mk.sp, mk.qsp, g["id_bf"], t["ident_bf"]))
    dep.append(mk.dma(mk.sp, mk.qsp, g["id_f32"], t["ident_f32"]))
    dep.append(mk.dma(mk.sp, mk.qsp, g["oc_f32"][0:1, :], t["ones_col_f32"]))
    dep.append(mk.dma(mk.sp, mk.qsp, g["oc_bf"][0:1, :], t["ones_col_bf"]))
    dep.append(mk.dma(mk.sp, mk.qsp, g["o128_f32"], t["ones128_f32"]))
    dep.append(mk.dma(mk.sp, mk.qsp, g["o128_bf"], t["ones128_bf"]))
    dep.append(mk.dma(mk.sp, mk.qsp, g["constf"], t["constf"]))
    ar.mark()
    aload = ar.alloc(32 * DS, F32)
    tk = mk.dma(mk.sp, mk.qsp, aload, t["A_log"])
    etk = mk.ev(mk.act, lambda e, o=g["negA"], i=aload:
                e.activation(o, i, AF.Exp), deps=[tk])
    g["negA_tok"] = mk.ev(mk.dve, lambda e, o=g["negA"]:
                          e.tensor_single_scalar(o, o, -1.0, Op.mult),
                          deps=[etk])
    ar.release()
    for s in (mk.pe, mk.dve, mk.act, mk.pool):
        mk.wait(s, (mk.qsp, mk.qsp.n))

    if phases == "LNONLY":
        ar.mark()
        lnt = [ar.alloc(512, BF16) for _ in range(16)]
        tk = _emit_ln(mk, t, ar, pb, g, t["x0T"], 0, lnt, 0)
        for i in range(16):
            mk.dma(mk.pool, mk.qpl, t["outT"][i * 128:(i + 1) * 128, 0:512],
                   lnt[i], deps=[tk])
        ar.release()
        for q in (mk.qsp, mk.qpl):
            mk.wait(mk.sp, (q, q.n))
        return
    g["pc"] = {}
    for snm, inm in (("WqS", "Wq"), ("WkS", "Wk"), ("WsS", "Ws"),
                     ("WvS", "Wv"), ("WgS", "Wg"), ("WoS", "Wo"),
                     ("WinS", "W_inT"), ("WxS", "W_xT"),
                     ("WdtS", "W_dtT"), ("gusS", "gus"), ("dnsS", "dns")):
        g["pc"][snm] = mk.dma(mk.pool, mk.qpl, t[snm], t[inm])
    g["rep8"] = None
    chain = [p for p in ("A", "B", "C", "D") if p in phases]
    if "P0" in phases:
        _p0_dequant(mk, t, ar, g, pb)
    cur_src = "x0T"
    cur_tok = None
    outs = {"A": "x1T", "B": "x2T", "C": "x3T", "D": "outT"}
    for i, p in enumerate(chain):
        dst = "outT" if i == len(chain) - 1 else outs[p]
        if p == "A":
            _phase_attn_a(mk, t, ar, pb, g, t[cur_src], t[dst])
            cur_tok = g["x1_tok"]
        elif p in ("B", "D"):
            cur_tok = _phase_ffn(mk, t, ar, pb, g, t[cur_src], t[dst],
                                 in_tok=cur_tok)
        elif p == "C":
            _phase_mamba(mk, t, ar, pb, g, t[cur_src], t[dst],
                         in_tok=cur_tok)
            cur_tok = g["x3_tok"]
        cur_src = dst
    if cur_tok is not None:
        mk.final_deps.append(cur_tok)

    for q in (mk.qsp, mk.qpl):
        mk.wait(mk.sp, (q, q.n))
    for s in (mk.pe, mk.dve, mk.act, mk.pool):
        mk.wait(mk.sp, (s, s.n))


def _p0_dequant(mk, t, ar, g, pb):
    mk.phase_barrier()
    OB = 1024
    ar.mark()
    rep8 = ar.alloc(128, F32)
    rtk = mk.dma(mk.sp, mk.qsp, rep8[0:8, :], t["rep8"])
    pk = [ar.alloc(OB, I32) for _ in range(2)]
    sc = [ar.alloc(OB, F32) for _ in range(2)]
    srow = [ar.alloc(OB, F32) for _ in range(2)]
    dq = [ar.alloc(8 * OB, BF16) for _ in range(2)]
    tmp = {nm: ar.alloc(OB, I32)
           for nm in ("nib", "a2", "mb", "mb22", "sb", "bits")}

    jobs = []
    for (pkn, scn, outn, n_ob, n_wt, wlast) in (
            ("gupT", "gusS", "wguS", GU_OUT // OB, 2, 128),
            ("dnpT", "dnsS", "wdnS", DM // OB, 6, 64)):
        for ob in range(n_ob):
            for wt in range(n_wt):
                jobs.append((t[pkn], t[scn], t[outn], ob, wt,
                             wlast if wt == n_wt - 1 else 128))

    slot_free = [None, None]
    wtoks = []
    for ji, (pkT, scT, outS, ob, wt, P) in enumerate(jobs):
        s = ji % 2
        deps = [] if slot_free[s] is None else [slot_free[s]]
        tk_p = mk.dma(mk.sp, mk.qsp, pk[s][0:P, :],
                      pkT[wt * 128:wt * 128 + P, ob * OB:(ob + 1) * OB],
                      deps=deps)
        nsr = P // 16
        sc_pc = g["pc"]["gusS" if scT is t["gusS"] else "dnsS"]
        tk_r = mk.dma(mk.sp, mk.qsp, srow[s][0:nsr, :],
                      scT[wt * 8:wt * 8 + nsr, ob * OB:(ob + 1) * OB],
                      deps=deps + [sc_pc])
        for half in range(2):
            mk.mm(pb[2 + s][0:P, 0:512] if half == 0 else
                  pb[2 + s][0:P, 0:512],
                  rep8[0:nsr, 0:P],
                  srow[s][0:nsr, half * 512:(half + 1) * 512],
                  True, True, deps=[tk_r, rtk], inc=(half == 1))                 if False else None
        mtk = None
        for half in range(2):
            mtk = mk.mm(pb[2 + s][0:P, 0:512], rep8[0:nsr, 0:P],
                        srow[s][0:nsr, half * 512:(half + 1) * 512],
                        True, True, deps=[tk_r, rtk], inc=True)
            etk = mk.ev(mk.dve, lambda e,
                        o=sc[s][0:P, half * 512:(half + 1) * 512],
                        p=pb[2 + s][0:P, 0:512]: e.tensor_copy(o, p),
                        deps=[mtk])
            mk.wait(mk.pe, etk)
        tk_s = mk.dve.tok()
        last = None
        for j in range(8):
            def seq(e, w=pk[s][0:P, :], j=j,
                    nib=tmp["nib"][0:P, :], a2=tmp["a2"][0:P, :],
                    mb=tmp["mb"][0:P, :], mb22=tmp["mb22"][0:P, :],
                    sb=tmp["sb"][0:P, :], bt=tmp["bits"][0:P, :],
                    scv=sc[s][0:P, :], dqv=dq[s][0:P, j * OB:(j + 1) * OB]):
                e.tensor_scalar(nib, w, 4 * j, 7,
                                Op.logical_shift_right, Op.bitwise_and)
                e.tensor_single_scalar(a2, nib, 252, Op.mult)
                e.scalar_tensor_tensor(mb, nib, 252, a2, Op.add, Op.min)
                e.tensor_single_scalar(mb22, mb, 22, Op.logical_shift_left)
                e.tensor_scalar(sb, w, 28 - 4 * j, -2147483648,
                                Op.logical_shift_left, Op.bitwise_and)
                e.tensor_tensor(bt, sb, mb22, Op.bitwise_or)
                return e.tensor_tensor(dqv, bt.bitcast(F32), scv, Op.mult)
            dd = [tk_p, tk_s]
            if ji >= 2:
                dd.append(wtoks[ji - 2])
            last = mk.ev(mk.dve, seq, deps=dd)
        dst = outS[wt * 1024:wt * 1024 + 8 * P, ob * OB:(ob + 1) * OB] \
            .rearrange("(w e) n -> w e n", e=8)
        wtk = mk.dma(mk.sp, mk.qsp, dst,
                     dq[s][0:P, :].rearrange("p (e n) -> p e n", e=8),
                     deps=[last])
        wtoks.append(wtk)
        slot_free[s] = last
    g["dequant_done"] = wtoks[-1]
    ar.release()


def _emit_ln(mk, t, ar, pb, g, src, t0, out_tiles, out_off, in_tok=None,
             xt_ext=None):
    Tc = 512
    ar.mark()
    xt = xt_ext if xt_ext is not None else \
        [ar.alloc(Tc, F32) for _ in range(16)]
    sq = ar.alloc(Tc, F32)
    rows = ar.alloc(4 * Tc, F32)
    rstd_b = ar.alloc(Tc, F32)
    nmr_b = ar.alloc(Tc, F32)

    deps = [in_tok] if in_tok is not None else []
    prev_ln = getattr(mk, "_last_ln_apply", None)
    if prev_ln is not None:
        deps.append(prev_ln)
    xtoks = [mk.dma(mk.sp, mk.qsp, xt[i],
                    src[i * 128:(i + 1) * 128, t0:t0 + Tc], deps=deps)
             for i in range(16)]
    tsum = None
    for i in range(16):
        tsum = mk.mm(pb[0][0:1, 0:Tc], g["o128_f32"], xt[i],
                     i == 0, i == 15, deps=[xtoks[i]], inc=(i == 15))
    tsq = None
    for i in range(16):
        sqt = mk.ev(mk.act, lambda e, o=sq, i_=xt[i]:
                    e.activation(o, i_, AF.Square), deps=[xtoks[i]])
        tsq = mk.mm(pb[1][0:1, 0:Tc], g["o128_f32"], sq, i == 0, i == 15,
                    deps=[sqt], inc=True)
        if i < 15:
            mk.wait(mk.act, tsq)
    nm = rows[0:1, 0:Tc]
    ex2 = rows[0:1, Tc:2 * Tc]
    rstd = rows[0:1, 2 * Tc:3 * Tc]
    nmr = rows[0:1, 3 * Tc:4 * Tc]

    def stats(e, nm=nm, ex2=ex2, rstd=rstd, s0=pb[0][0:1, 0:Tc],
              s1=pb[1][0:1, 0:Tc]):
        e.tensor_single_scalar(nm, s0, -1.0 / DM, Op.mult)
        e.tensor_single_scalar(ex2, s1, 1.0 / DM, Op.mult)
        e.tensor_tensor(rstd, nm, nm, Op.mult)
        return e.tensor_tensor(ex2, ex2, rstd, Op.subtract)
    vtok = mk.ev(mk.dve, stats, deps=[tsum, tsq])

    def rstdf(e, ex2=ex2, rstd=rstd):
        e.activation(rstd, ex2, AF.Ln, bias=1e-5)
        return e.activation(rstd, rstd, AF.Exp, scale=-0.5)
    rtok = mk.ev(mk.act, rstdf, deps=[vtok])
    ntok = mk.ev(mk.dve, lambda e, nmr=nmr, nm=nm, rstd=rstd:
                 e.tensor_tensor(nmr, nm, rstd, Op.mult), deps=[rtok])
    mk.mm(pb[2][:, 0:Tc], g["oc_f32"][0:1, :], rstd, True, True,
          deps=[rtok])
    btok = mk.mm(pb[3][:, 0:Tc], g["oc_f32"][0:1, :], nmr, True, True,
                 deps=[ntok], inc=True)
    ctok = mk.ev(mk.dve, lambda e, a=rstd_b, b=nmr_b, p2=pb[2][:, 0:Tc],
                 p3=pb[3][:, 0:Tc]:
                 (e.tensor_copy(a, p2), e.tensor_copy(b, p3))[-1],
                 deps=[btok])
    out_tok = None
    for i in range(16):
        def app(e, x=xt[i], a=rstd_b, b=nmr_b, s=sq,
                o=out_tiles[i][:, out_off:out_off + Tc]):
            e.tensor_tensor(s, x, a, Op.mult)
            return e.tensor_tensor(o, s, b, Op.add)
        out_tok = mk.ev(mk.dve, app, deps=[ctok])
    mk.wait(mk.pe, ctok)
    mk._last_ln_apply = out_tok
    ar.release()
    return out_tok


def _phase_attn_a(mk, t, ar, pb, g, src, dst):
    mk.phase_barrier()
    TQ = 512
    ar.mark()
    hk_f = ar.alloc(HEADS * M, F32)
    hk_b = ar.alloc(HEADS * M, BF16)
    hv_f = ar.alloc(HEADS * DV, F32)
    hv_b = ar.alloc(HEADS * DV, BF16)
    zc_f = ar.alloc(HEADS * M, F32)
    zc_b = ar.alloc(HEADS * M, BF16)
    for a in (hk_f, hk_b, hv_f, hv_b, zc_f):
        mk.ev(mk.dve, lambda e, a=a: e.memset(a, 0.0), inc=False)
    zt = mk.dve.inc(lambda e, a=zc_b: e.memset(a, 0.0))
    mk.wait(mk.pe, zt)

    ln1 = [ar.alloc(TQ, BF16) for _ in range(16)]
    qT = [ar.alloc(TQ, BF16) for _ in range(16)]
    kT = [ar.alloc(TQ, BF16) for _ in range(16)]
    Efm = [ar.alloc(TQ, BF16) for _ in range(16)]
    vgreg = ar.alloc(16 * TQ, I32)
    Vfm = [vgreg[:, i * 256:(i + 1) * 256].bitcast(BF16)
           for i in range(16)]
    Gs = [vgreg[:, 4096 + i * 256:4096 + (i + 1) * 256].bitcast(BF16)
          for i in range(16)]
    xt16 = [vgreg[:, i * 512:(i + 1) * 512].bitcast(F32)
            for i in range(16)]
    ovg = [ar.alloc(TQ, BF16) for _ in range(16)]
    wbuf = [ar.alloc(16 * 128, BF16) for _ in range(2)]
    clip = [ar.alloc(TQ, F32) for _ in range(2)]
    amt = ar.alloc(128, BF16)
    ktm = ar.alloc(64, BF16)
    etm = ar.alloc(64, BF16)
    vtm = ar.alloc(128, BF16)
    rfm = ar.alloc(128, BF16)
    rcpZ = ar.alloc(64, F32)
    okb = ar.alloc(64, F32)
    mx = ar.alloc(2, F32)
    pv = ar.alloc(64, F32)
    se = ar.alloc(2, F32)
    r1 = ar.alloc(64, F32)
    rbf = ar.alloc(64, BF16)
    ov_sb = ar.alloc(TQ, F32)
    ovsq = ar.alloc(TQ, F32)
    rows = ar.alloc(TQ, F32)
    rstd_b = ar.alloc(TQ, F32)
    x0t = ar.alloc(TQ, F32)
    yout = ar.alloc(TQ, F32)

    wplan = ([("s", "WsS", i) for i in range(8)] +
             [("v", "WvS", i) for i in range(16)] +
             [("q", "WqS", i) for i in range(8)] +
             [("k", "WkS", i) for i in range(8)] +
             [("g", "WgS", i) for i in range(16)])

    x1_tok = None
    for Qi in range(4):
        t0 = Qi * TQ
        lntok = _emit_ln(mk, t, ar, pb, g, src, t0, ln1, 0, xt_ext=xt16,
                         in_tok=x1_tok)
        wfree = [None, None]
        for wi, (kind, wnm, mi) in enumerate(wplan):
            s = wi % 2
            deps = [lntok, g["pc"][wnm]]
            if wfree[s] is not None:
                deps.append(wfree[s])
            wtok = mk.dma(mk.sp, mk.qsp,
                          wbuf[s].rearrange("p (c m) -> p c m", c=16),
                          t[wnm][:, mi * 128:(mi + 1) * 128]
                          .rearrange("(c p) m -> p c m", p=128), deps=deps)
            if kind in ("q", "k", "s"):
                banks = (pb[4 + 2 * s], pb[5 + 2 * s])
                ptok = None
                for par in range(2):
                    hh = 2 * mi + par
                    bank = banks[par]
                    for kc in range(16):
                        ptok = mk.mm(bank[0:64, 0:TQ],
                                     wbuf[s][:, kc * 128 + par * 64:
                                             kc * 128 + par * 64 + 64],
                                     ln1[kc], kc == 0, kc == 15,
                                     deps=[wtok] if kc == 0 else (),
                                     inc=(par == 1 and kc == 15))
                toks = []
                for par in range(2):
                    hh = 2 * mi + par
                    bank = banks[par]
                    if kind == "s":
                        ctk = mk.ev(mk.dve, lambda e, c=clip[s][0:64, :],
                                    p=bank[0:64, 0:TQ]:
                                    e.tensor_scalar(c, p, 32.0, -32.0,
                                                    Op.min, Op.max),
                                    deps=[ptok] + (toks[:1] if par else []))
                        toks.append(mk.ev(
                            mk.act, lambda e, o=Efm[hh][0:64, :],
                            c=clip[s][0:64, :]:
                            e.activation(o, c, AF.Exp), deps=[ctk]))
                    elif kind == "q":
                        toks.append(mk.ev(
                            mk.act, lambda e, o=qT[hh][0:64, :],
                            p=bank[0:64, 0:TQ]:
                            e.activation(o, p, AF.Copy, scale=0.125),
                            deps=[ptok]))
                    else:
                        toks.append(mk.ev(
                            mk.dve, lambda e, o=kT[hh][0:64, :],
                            p=bank[0:64, 0:TQ]:
                            e.tensor_copy(o, p), deps=[ptok]))
                for tkx in toks:
                    mk.wait(mk.pe, tkx)
                wfree[s] = toks[-1]
            else:
                bank = pb[4 + 2 * s]
                ptok = None
                for kc in range(16):
                    ptok = mk.mm(bank[:, 0:TQ],
                                 wbuf[s][:, kc * 128:(kc + 1) * 128],
                                 ln1[kc], kc == 0, kc == 15,
                                 deps=[wtok] if kc == 0 else (),
                                 inc=(kc == 15))
                if kind == "v":
                    wfree[s] = mk.ev(mk.dve, lambda e, o=Vfm[mi],
                                     p=bank[:, 0:TQ]:
                                     e.tensor_copy(o, p), deps=[ptok])
                else:
                    wfree[s] = mk.ev(mk.act, lambda e, o=Gs[mi],
                                     p=bank[:, 0:TQ]:
                                     e.activation(o, p, AF.Silu),
                                     deps=[ptok])
                mk.wait(mk.pe, wfree[s])
        p6b = pb[6].bitcast(BF16)
        p7b = pb[7].bitcast(BF16)
        for h in range(HEADS):
            qh = qT[h][0:64, :]
            kh = kT[h][0:64, :]
            eh = Efm[h][0:64, :]
            vh = Vfm[h]
            hkf = hk_f[0:64, h * M:(h + 1) * M]
            hkb = hk_b[0:64, h * M:(h + 1) * M]
            hvf = hv_f[0:64, h * DV:(h + 1) * DV]
            hvb = hv_b[0:64, h * DV:(h + 1) * DV]
            zcf = zc_f[0:1, h * M:(h + 1) * M]
            zcb = zc_b[0:1, h * M:(h + 1) * M]
            ovc_tok = None
            for c in range(4):
                cs = slice(c * 128, (c + 1) * 128)
                attok = mk.mm(pb[0][:, 0:128], kh[:, cs], qh[:, cs],
                              True, True, inc=True)
                mk.tp(p6b[:, 0:64], kh[:, cs], g["id_bf"][0:64, 0:64])
                mk.tp(p6b[:, 64:128], eh[:, cs], g["id_bf"][0:64, 0:64])
                ttok = mk.tp(p6b[:, 128:256], vh[:, cs], g["id_bf"],
                             inc=True)
                amtok = mk.ev(mk.dve, lambda e, o=amt, p=pb[0][:, 0:128],
                              m=g["triu"]:
                              e.tensor_tensor(o, p, m, Op.mult),
                              deps=[attok])

                def ev4(e, a=ktm, b=etm, c2=vtm, p=p6b):
                    e.tensor_copy(a, p[:, 0:64])
                    e.tensor_copy(b, p[:, 64:128])
                    return e.tensor_copy(c2, p[:, 128:256])
                ttok2 = mk.ev(mk.dve, ev4, deps=[ttok])
                mk.mm(pb[1][:, 0:64], amt, etm, True, False,
                      deps=[amtok, ttok2])
                ntok = mk.mm(pb[1][:, 0:64], qh[:, cs], hkb, False, True,
                             inc=True)
                mk.mm(pb[2][:, 0:64], g["triu"], etm, True, False)
                ztok = mk.mm(pb[2][:, 0:64], g["oc_bf"][0:1, :], zcb,
                             False, True, inc=True)

                def ev78(e, rz=rcpZ, pz=pb[2][:, 0:64], ok=okb,
                         pn=pb[1][:, 0:64], m=mx[:, 0:1]):
                    e.reciprocal(rz, pz)
                    e.tensor_tensor(ok, pn, rz, Op.mult)
                    return e.tensor_reduce(m, ok, axis=mybir.AxisListType.X,
                                           op=Op.max, negate=True)
                oktok = mk.ev(mk.dve, ev78, deps=[ntok, ztok])
                pvtok = mk.ev(mk.act, lambda e, o=pv, i=okb, bb=mx[:, 0:1],
                              a=se[:, 0:1]:
                              e.activation(o, i, AF.Exp, bias=bb,
                                           accum_out=a), deps=[oktok])

                def ev10(e, rs=se[:, 0:1], p=pv, r_=r1, rz=rcpZ, rb=rbf,
                         m2=mx[:, 1:2]):
                    e.reciprocal(m2, rs)
                    e.tensor_scalar(r_, p, m2, 1.0, Op.mult, Op.mult)
                    return e.tensor_tensor(rb, r_, rz, Op.mult)
                rtok = mk.ev(mk.dve, ev10, deps=[pvtok])
                trtok = mk.tp(p7b[0:64, 0:128], rbf, g["id_bf"],
                              deps=[rtok], inc=True)
                rftok = mk.ev(mk.dve, lambda e, o=rfm[0:64, :],
                              p=p7b[0:64, 0:128]:
                              e.tensor_copy(o, p), deps=[trtok])
                pttok = mk.mm(pb[0][:, 128:256], eh[:, cs], rfm[0:64, :],
                              True, True, deps=[rftok], inc=True)
                ptmtok = mk.ev(mk.dve, lambda e, o=amt, p=pb[0][:, 128:256],
                               m=g["triu"]:
                               e.tensor_tensor(o, p, m, Op.mult),
                               deps=[pttok])
                mk.mm(pb[3][:, 0:128], vtm, amt, True, False, deps=[ptmtok])
                ovtok = mk.mm(pb[3][:, 0:128], hvb, rfm[0:64, :],
                              False, True, inc=True)
                mk.mm(pb[4][0:64, 0:64], ktm, etm, True, True)
                mk.mm(pb[4][0:1, 448:512], g["o128_bf"], etm, True, True)
                sdtok = mk.mm(pb[5][0:64, 0:128], etm, vtm, True, True,
                              inc=True)

                def ev18(e, hkf=hkf, hkb=hkb, hvf=hvf, hvb=hvb, zcf=zcf,
                         zcb=zcb, d4=pb[4][0:64, 0:64],
                         dz=pb[4][0:1, 448:512], d5=pb[5][0:64, 0:128]):
                    e.tensor_tensor(hkf, hkf, d4, Op.add)
                    e.tensor_copy(hkb, hkf)
                    e.tensor_tensor(hvf, hvf, d5, Op.add)
                    e.tensor_copy(hvb, hvf)
                    e.tensor_tensor(zcf, zcf, dz, Op.add)
                    return e.tensor_copy(zcb, zcf)
                utok = mk.ev(mk.dve, ev18, deps=[sdtok])
                ovc_tok = mk.ev(mk.dve, lambda e, o=ov_sb[:, cs],
                                p=pb[3][:, 0:128]: e.tensor_copy(o, p),
                                deps=[ovtok])
                mk.wait(mk.pe, utok)
                mk.wait(mk.pe, ovc_tok)
            sqtok = mk.ev(mk.act, lambda e, o=ovsq, i=ov_sb:
                          e.activation(o, i, AF.Square), deps=[ovc_tok])
            sstok = mk.mm(pb[0][0:1, 0:TQ], g["o128_f32"], ovsq, True, True,
                          deps=[sqtok], inc=True)

            def rstdf(e, r=rows[0:1, 0:TQ], p=pb[0][0:1, 0:TQ]):
                e.activation(r, p, AF.Ln, scale=1.0 / DV, bias=1e-5)
                return e.activation(r, r, AF.Exp, scale=-0.5)
            rtok2 = mk.ev(mk.act, rstdf, deps=[sstok])
            btok = mk.mm(pb[3][:, 0:TQ], g["oc_f32"][0:1, :],
                         rows[0:1, 0:TQ], True, True, deps=[rtok2], inc=True)
            ctok = mk.ev(mk.dve, lambda e, a=rstd_b, p=pb[3][:, 0:TQ]:
                         e.tensor_copy(a, p), deps=[btok])

            def gate(e, o=ovg[h], ov=ov_sb, gn=g["gnw"][:, 0:1], rb=rstd_b,
                     gs=Gs[h], s=ovsq):
                e.scalar_tensor_tensor(s, ov, gn, rb, Op.mult, Op.mult)
                return e.tensor_tensor(o, s, gs, Op.mult)
            gtok = mk.ev(mk.dve, gate, deps=[ctok])
            mk.wait(mk.pe, gtok)
        wofree = [mk.pe.tok(), mk.pe.tok()]
        for dtile in range(16):
            s = dtile % 2
            deps = [wofree[s], g["pc"]["WoS"]]
            wtok = mk.dma(mk.sp, mk.qsp,
                          wbuf[s].rearrange("p (c m) -> p c m", c=16),
                          t["WoS"][:, dtile * 128:(dtile + 1) * 128]
                          .rearrange("(c p) m -> p c m", p=128), deps=deps)
            xtok = mk.dma(mk.sp, mk.qsp, x0t,
                          src[dtile * 128:(dtile + 1) * 128, t0:t0 + TQ],
                          deps=[x1_tok] if x1_tok is not None else [])
            bank = pb[1 + s]
            ptok = None
            for vc in range(16):
                ptok = mk.mm(bank[:, 0:TQ],
                             wbuf[s][:, vc * 128:(vc + 1) * 128],
                             ovg[vc], vc == 0, vc == 15,
                             deps=[wtok] if vc == 0 else (),
                             inc=(vc == 15))
            otok = mk.ev(mk.dve, lambda e, o=yout, p=bank[:, 0:TQ], x=x0t:
                         e.tensor_tensor(o, p, x, Op.add),
                         deps=[ptok, xtok])
            x1_tok = mk.dma(mk.sp, mk.qsp,
                            dst[dtile * 128:(dtile + 1) * 128, t0:t0 + TQ],
                            yout, deps=[otok])
            mk.wait(mk.dve, x1_tok)
            mk.wait(mk.pe, otok)
            wofree[s] = mk.pe.tok()
    g["x1_tok"] = x1_tok
    ar.release()


def _phase_ffn(mk, t, ar, pb, g, src, dst, in_tok):
    mk.phase_barrier()
    trunc = g.get("trunc", 9)
    NG = FFN_I // 128
    out_tok = None
    for oc in range(2):
        mk.phase_barrier()
        t0 = oc * 1024
        ar.mark()
        h = [ar.alloc(1024, BF16) for _ in range(NG)]
        gs_tmp = ar.alloc(1024, BF16)
        wbuf = [ar.alloc(16 * 128, BF16) for _ in range(4)]
        ar.mark()
        ln2 = [ar.alloc(1024, BF16) for _ in range(16)]
        lt1 = _emit_ln(mk, t, ar, pb, g, src, t0, ln2, 0, in_tok=in_tok)
        lt2 = _emit_ln(mk, t, ar, pb, g, src, t0 + 512, ln2, 512,
                       in_tok=in_tok)
        if g.get("dbgln"):
            for i in range(16):
                mk.dma(mk.pool, mk.qpl,
                       t["x1T"][i * 128:(i + 1) * 128, t0:t0 + 1024],
                       ln2[i], deps=[lt1, lt2])
        dq = g.get("dequant_done")
        wfree = [None] * 4
        bankfree = [None, None]
        jobs = []
        for bi in range(NG):
            jobs.append((0, bi))
            jobs.append((1, bi + NG))
        last_up = None
        last_gate = None
        for ji, (half, mi) in enumerate(jobs):
            sl = ji % 4
            pr = ji % 2
            bi = mi % NG
            deps = [lt1, lt2]
            if wfree[sl] is not None:
                deps.append(wfree[sl])
            if dq is not None:
                deps.append(dq)
            wtok = mk.dma(mk.sp, mk.qsp,
                          wbuf[sl].rearrange("p (c m) -> p c m", c=16),
                          t["wguS"][:, mi * 128:(mi + 1) * 128]
                          .rearrange("(c p) m -> p c m", p=128),
                          deps=deps)
            if bankfree[pr] is not None:
                mk.wait(mk.pe, bankfree[pr])
            ptok = None
            for tc in range(2):
                bank = pb[4 + 2 * pr + tc]
                for kc in range(16):
                    ptok = mk.mm(bank[:, 0:512],
                                 wbuf[sl][:, kc * 128:(kc + 1) * 128],
                                 ln2[kc][:, tc * 512:(tc + 1) * 512],
                                 kc == 0, kc == 15,
                                 deps=[wtok] if kc == 0 else (),
                                 inc=(tc == 1 and kc == 15))
            if half == 0:
                def ev_g(e, o=gs_tmp, p0=pb[4 + 2 * pr][:, 0:512],
                         p1=pb[5 + 2 * pr][:, 0:512]):
                    e.activation(o[:, 0:512], p0, AF.Silu)
                    return e.activation(o[:, 512:1024], p1, AF.Silu)
                tk = mk.ev(mk.act, ev_g, deps=[ptok] +
                           ([last_up] if last_up is not None else []))
                last_gate = tk
            else:
                def ev_u(e, o=h[bi], gsv=gs_tmp,
                         p0=pb[4 + 2 * pr][:, 0:512],
                         p1=pb[5 + 2 * pr][:, 0:512]):
                    e.tensor_tensor(o[:, 0:512], p0, gsv[:, 0:512],
                                    Op.mult)
                    return e.tensor_tensor(o[:, 512:1024], p1,
                                           gsv[:, 512:1024], Op.mult)
                tk = mk.ev(mk.dve, ev_u, deps=[ptok, last_gate])
                last_up = tk
            wfree[sl] = tk
            bankfree[pr] = tk
        gu_done = [mk.dve.tok(), mk.act.tok()]
        if trunc < 2:
            ar.release()
            ar.release()
            return None
        ar.release()
        ysb = [ar.alloc(1024, BF16) for _ in range(16)]
        wdbuf = [ar.alloc(NG * 128, BF16) for _ in range(3)]
        ysq = ar.alloc(1024, F32)
        rows = ar.alloc(1024, F32)
        rstd_b = ar.alloc(1024, F32)
        x1t = [ar.alloc(1024, F32) for _ in range(2)]
        yt = ar.alloc(1024, F32)
        wfree = [None, None, None]
        bankfree = [None, None]
        for dtile in range(16):
            sl = dtile % 3
            s = dtile % 2
            deps = [] if wfree[sl] is None else [wfree[sl]]
            if dq is not None:
                deps.append(dq)
            wtok = mk.dma(mk.sp, mk.qsp,
                          wdbuf[sl].rearrange("p (c m) -> p c m", c=NG),
                          t["wdnS"][:, dtile * 128:(dtile + 1) * 128]
                          .rearrange("(c p) m -> p c m", p=128), deps=deps)
            if bankfree[s] is not None:
                mk.wait(mk.pe, bankfree[s])
            ptok = None
            for tc in range(2):
                bank = pb[0 + 2 * s + tc]
                for kc in range(NG):
                    ptok = mk.mm(bank[:, 0:512],
                                 wdbuf[sl][:, kc * 128:(kc + 1) * 128],
                                 h[kc][:, tc * 512:(tc + 1) * 512],
                                 kc == 0, kc == NG - 1,
                                 deps=[wtok] if kc == 0 else (),
                                 inc=(tc == 1 and kc == NG - 1))

            def ev_y(e, o=ysb[dtile], q0=ysq, p0=pb[0 + 2 * s][:, 0:512],
                     p1=pb[1 + 2 * s][:, 0:512]):
                e.activation(o[:, 0:512], p0, AF.Copy)
                e.activation(o[:, 512:1024], p1, AF.Copy)
                e.activation(q0[:, 0:512], p0, AF.Square)
                return e.activation(q0[:, 512:1024], p1, AF.Square)
            ytok = mk.ev(mk.act, ev_y, deps=[ptok])
            wfree[sl] = ytok
            bankfree[s] = ytok
            mk.mm(pb[6][0:1, 0:512], g["o128_f32"], ysq[:, 0:512],
                  dtile == 0, dtile == 15,
                  deps=[ytok] + (gu_done if dtile == 0 else []))
            nptok = mk.mm(pb[7][0:1, 0:512], g["o128_f32"],
                          ysq[:, 512:1024], dtile == 0, dtile == 15,
                          inc=True)
            mk.wait(mk.act, nptok)
        sstok = mk.pe.tok()
        if trunc < 3:
            ar.release()
            return None

        def rstdf(e, r=rows[0:1, :], p6=pb[6][0:1, 0:512],
                  p7=pb[7][0:1, 0:512]):
            e.activation(r[0:1, 0:512], p6, AF.Ln, scale=1.0 / DM, bias=1e-6)
            e.activation(r[0:1, 512:1024], p7, AF.Ln, scale=1.0 / DM,
                         bias=1e-6)
            return e.activation(r, r, AF.Exp, scale=-0.5)
        rtok = mk.ev(mk.act, rstdf, deps=[sstok])
        mk.mm(pb[2][:, 0:512], g["oc_f32"][0:1, :], rows[0:1, 0:512],
              True, True, deps=[rtok])
        btok = mk.mm(pb[3][:, 0:512], g["oc_f32"][0:1, :],
                     rows[0:1, 512:1024], True, True, inc=True)
        ctok = mk.ev(mk.dve, lambda e, a=rstd_b, p2=pb[2][:, 0:512],
                     p3=pb[3][:, 0:512]:
                     (e.tensor_copy(a[:, 0:512], p2),
                      e.tensor_copy(a[:, 512:1024], p3))[-1], deps=[btok])
        prev_ot = [None, None]
        for dtile in range(16):
            s = dtile % 2
            xdeps = [in_tok] if in_tok is not None else []
            if prev_ot[s] is not None:
                xdeps.append(prev_ot[s])
            xtok = mk.dma(mk.sp, mk.qsp, x1t[s],
                          src[dtile * 128:(dtile + 1) * 128, t0:t0 + 1024],
                          deps=xdeps)

            def out_ev(e, y=ysb[dtile], rb=rstd_b, x=x1t[s], o=yt):
                e.tensor_tensor(o, y, rb, Op.mult)
                return e.tensor_tensor(o, o, x, Op.add)
            otok = mk.ev(mk.dve, out_ev, deps=[ctok, xtok])
            prev_ot[s] = otok
            out_tok = mk.dma(mk.sp, mk.qsp,
                             dst[dtile * 128:(dtile + 1) * 128, t0:t0 + 1024],
                             yt, deps=[otok])
            mk.wait(mk.dve, out_tok)
        ar.release()
    return out_tok


def _phase_mamba(mk, t, ar, pb, g, src, dst, in_tok):
    mk.phase_barrier()
    ar.mark()
    WdtT = ar.alloc(DI, BF16)
    state = ar.alloc(32 * DS, F32)
    wdt_tok = mk.dma(mk.sp, mk.qsp, WdtT, t["WdtS"],
                     deps=[g["pc"]["WdtS"]])
    szero = mk.ev(mk.dve, lambda e, a=state: e.memset(a, 0.0))
    wout_tok = mk.dma(mk.pool, mk.qpl, t["woutS"], t["W_outT"])

    ar.mark()
    WxT = ar.alloc(32 * XDBL, BF16)
    wx_tok = mk.dma(mk.sp, mk.qsp,
                    WxT.rearrange("p (c k) -> p c k", c=32),
                    t["WxS"].rearrange("(c p) k -> p c k", p=128),
                    deps=[g["pc"]["WxS"]])
    xdbl = [ar.alloc(XDBL, F32) for _ in range(16)]
    xz0 = None
    for i in range(16):
        xz0 = mk.ev(mk.dve, lambda e, a=xdbl[i]: e.memset(a, 0.0),
                    inc=(i == 15))
    ln3 = [ar.alloc(T, BF16) for _ in range(16)]
    lts = [_emit_ln(mk, t, ar, pb, g, src, q * 512, ln3, q * 512,
                    in_tok=in_tok) for q in range(4)]
    ar.mark()
    wbuf = [ar.alloc(16 * 128, BF16) for _ in range(2)]
    xc = ar.alloc(T, F32)
    u0 = ar.alloc(T, F32)
    u1 = ar.alloc(T, F32)
    ut = ar.alloc(T, BF16)
    szt = ar.alloc(T, BF16)
    wfree = [None, None]
    u_tok = None
    sz_tok = None
    for ob in range(64):
        s = ob % 2
        deps = list(lts) if wfree[s] is None else [wfree[s]] + list(lts)
        wtok = mk.dma(mk.sp, mk.qsp,
                      wbuf[s].rearrange("p (c m) -> p c m", c=16),
                      t["WinS"][:, ob * 128:(ob + 1) * 128]
                      .rearrange("(c p) m -> p c m", p=128),
                      deps=deps + [g["pc"]["WinS"]])
        ev_toks = []
        for tc in range(4):
            bank = pb[2 * s + (tc % 2)]
            ptok = None
            for kc in range(16):
                ptok = mk.mm(bank[:, 0:512],
                             wbuf[s][:, kc * 128:(kc + 1) * 128],
                             ln3[kc][:, tc * 512:(tc + 1) * 512],
                             kc == 0, kc == 15,
                             deps=[wtok] if kc == 0 else (),
                             inc=(kc == 15))
            if ob < 32:
                cp = mk.ev(mk.dve,
                           lambda e, o=xc[:, tc * 512:(tc + 1) * 512],
                           p=bank[:, 0:512]: e.tensor_copy(o, p),
                           deps=[ptok])
            else:
                cp = mk.ev(mk.act,
                           lambda e, o=szt[:, tc * 512:(tc + 1) * 512],
                           p=bank[:, 0:512]: e.activation(o, p, AF.Silu),
                           deps=[ptok])
            ev_toks.append(cp)
            mk.wait(mk.pe, cp)
        if ob < 32:
            di = ob
            prev_sl = getattr(mk, "_c2_sltok", None)

            def conv(e, x=xc, a=u0, b=u1, cw=g["cw"], i=di):
                c3 = cw[:, i * 4 + 3:i * 4 + 4]
                c2 = cw[:, i * 4 + 2:i * 4 + 3]
                c1 = cw[:, i * 4 + 1:i * 4 + 2]
                c0 = cw[:, i * 4 + 0:i * 4 + 1]
                e.tensor_scalar(a, x, c3, 1.0, Op.mult, Op.mult)
                e.scalar_tensor_tensor(b[:, 1:T], x[:, 0:T - 1], c2,
                                       a[:, 1:T], Op.mult, Op.add)
                e.tensor_copy(b[:, 0:1], a[:, 0:1])
                e.scalar_tensor_tensor(a[:, 2:T], x[:, 0:T - 2], c1,
                                       b[:, 2:T], Op.mult, Op.add)
                e.tensor_copy(a[:, 0:2], b[:, 0:2])
                e.scalar_tensor_tensor(b[:, 3:T], x[:, 0:T - 3], c0,
                                       a[:, 3:T], Op.mult, Op.add)
                return e.tensor_copy(b[:, 0:3], a[:, 0:3])
            cvtok = mk.ev(mk.dve, conv,
                          deps=[ev_toks[-1]] +
                          ([prev_sl] if prev_sl is not None else []))
            sltok = mk.ev(mk.act, lambda e, o=ut, i=u1,
                          cb=g["cb"][:, di:di + 1]:
                          e.activation(o, i, AF.Silu, bias=cb),
                          deps=[cvtok])
            mk._c2_sltok = sltok
            u_tok = mk.dma(mk.sp, mk.qsp,
                           t["uS"][di * 128:(di + 1) * 128, :], ut,
                           deps=[sltok])
            for tt in range(16):
                bank = pb[4 + (tt % 2)]
                mtok = mk.mm(bank[:, 0:XDBL],
                             ut[:, tt * 128:(tt + 1) * 128],
                             WxT[:, di * XDBL:(di + 1) * XDBL], True, True,
                             deps=[sltok, wx_tok] if tt < 2 else (),
                             inc=True)
                addtok = mk.ev(mk.dve, lambda e, a=xdbl[tt],
                               p=bank[:, 0:XDBL]:
                               e.tensor_tensor(a, a, p, Op.add),
                               deps=[mtok, xz0])
                mk.wait(mk.pe, addtok)
            wfree[s] = (mk.qsp, u_tok[1])
            mk.wait(mk.dve, u_tok)
            mk.wait(mk.act, mk.pe.tok())
        else:
            sz_tok = mk.dma(mk.sp, mk.qsp,
                            t["szS"][(ob - 32) * 128:(ob - 31) * 128, :],
                            szt, deps=[ev_toks[-1]])
            wfree[s] = sz_tok
            mk.wait(mk.act, sz_tok)
    ar.release()
    mk.phase_barrier()
    p6f = pb[6]
    cvt = ar.alloc(128, BF16)
    bcv = ar.alloc(128, BF16)
    dtT_tok = None
    bc_tok = None
    for tt in range(16):
        tk1 = mk.tp(p6f[:, 0:128], xdbl[tt][:, 0:128], g["id_f32"],
                    inc=True)
        e1t = mk.ev(mk.dve, lambda e, o=cvt, p=p6f[:, 0:128]:
                    e.tensor_copy(o, p), deps=[tk1])
        dtT_tok = mk.dma(mk.sp, mk.qsp, t["dtTS"][:, tt * 128:(tt + 1) * 128],
                         cvt, deps=[e1t])
        mk.wait(mk.dve, dtT_tok)
        tk2 = mk.tp(p6f[0:32, 128:256], xdbl[tt][:, 128:160], g["id_f32"],
                    deps=[e1t], inc=True)
        e2t = mk.ev(mk.dve, lambda e, o=bcv[0:32, :], p=p6f[0:32, 128:256]:
                    e.tensor_copy(o, p), deps=[tk2])
        bc_tok = mk.dma(mk.sp, mk.qsp,
                        t["bcS"][:, tt * 128:(tt + 1) * 128],
                        bcv[0:32, :], deps=[e2t])
        mk.wait(mk.dve, bc_tok)
        mk.wait(mk.pe, (mk.dve, mk.dve.n))
    ar.release()
    mk.phase_barrier()
    ar.mark()
    Brep = ar.alloc(DS * 512, BF16)
    Crep = ar.alloc(DS * 512, BF16)
    dA = ar.alloc(DS * 512, F32)
    Obuf = ar.alloc(DS * 512, BF16)
    dbu = ar.alloc(DS * 512, BF16)
    delta2 = [ar.alloc(512, F32) for _ in range(2)]
    tA = ar.alloc(512, F32)
    tB = ar.alloc(512, F32)
    tC = ar.alloc(512, F32)
    du = ar.alloc(512, BF16)
    ubuf = [ar.alloc(512, BF16) for _ in range(2)]
    szbuf = [ar.alloc(512, BF16) for _ in range(2)]
    dtbuf = [ar.alloc(512, BF16) for _ in range(2)]
    ytile = ar.alloc(512, F32)
    ygt = [ar.alloc(512, BF16) for _ in range(2)]
    ybuf = ar.alloc(32 * 256, BF16)
    yhbuf = ar.alloc(32 * 256, BF16)
    hbuf = [ar.alloc(32 * 128, BF16) for _ in range(2)]
    wobuf = [ar.alloc(32 * 128, BF16) for _ in range(2)]
    x2t = [ar.alloc(256, F32) for _ in range(2)]
    outt = ar.alloc(256, F32)
    bcrow = [ar.alloc(512, BF16) for _ in range(2)]
    ocb = dbu

    y_toks = []
    had_jobs = []
    out_tok = None

    def emit_had(tcq, mk=mk):
        nonlocal out_tok, had_pe_last
        t0q = tcq * 256
        ytk = []
        mk.wait(mk.pool, (mk.qpl, mk.qpl.n))
        ydeps = list(y_toks[tcq // 2])
        if had_pe_last is not None:
            ydeps.append(had_pe_last)
        for i in range(32):
            ytk.append(mk.dma(mk.pool, mk.qpl,
                              ybuf[:, i * 256:(i + 1) * 256],
                              t["yS"][i * 128:(i + 1) * 128, t0q:t0q + 256],
                              deps=ydeps))
        ysafe = plsafe()
        for db in range(32):
            s = db % 2
            deps = [had_free[s]] if had_free[s] is not None else []
            htok = mk.dma(mk.pool, mk.qpl,
                          hbuf[s].rearrange("p (c m) -> p c m", c=32),
                          t["Hmat"][:, db * 128:(db + 1) * 128]
                          .rearrange("(c p) m -> p c m", p=128), deps=deps)
            hsafe = plsafe()
            bank = pb[4 + s]
            ptok = None
            for kc in range(32):
                ptok = mk.mm(bank[:, 0:256],
                             hbuf[s][:, kc * 128:(kc + 1) * 128],
                             ybuf[:, kc * 256:(kc + 1) * 256],
                             kc == 0, kc == 31,
                             deps=[hsafe, ysafe] if kc == 0 else (),
                             inc=(kc == 31))
            had_free[s] = mk.ev(mk.act, lambda e,
                                o=yhbuf[:, db * 256:(db + 1) * 256],
                                p=bank[:, 0:256]:
                                e.activation(o, p, AF.Copy),
                                deps=[ptok])
            mk.wait(mk.pe, had_free[s])
        for ob in range(16):
            s = ob % 2
            deps = [wo_mm[s], wout_tok] if wo_mm[s] is not None \
                else [wout_tok]
            wtok = mk.dma(mk.pool, mk.qpl,
                          wobuf[s].rearrange("p (c m) -> p c m", c=32),
                          t["woutS"][:, ob * 128:(ob + 1) * 128]
                          .rearrange("(c p) m -> p c m", p=128), deps=deps)
            xtok = mk.dma(mk.pool, mk.qpl, x2t[s],
                          src[ob * 128:(ob + 1) * 128, t0q:t0q + 256],
                          deps=[wo_dve[s]] if wo_dve[s] is not None else [])
            wsafe = plsafe()
            bank = pb[6 + s]
            ptok = None
            for kc in range(32):
                ptok = mk.mm(bank[:, 0:256],
                             wobuf[s][:, kc * 128:(kc + 1) * 128],
                             yhbuf[:, kc * 256:(kc + 1) * 256],
                             kc == 0, kc == 31,
                             deps=[wsafe, had_free[0], had_free[1]]
                             if kc == 0 else (),
                             inc=(kc == 31))
            otok = mk.ev(mk.dve, lambda e, o=outt, p=bank[:, 0:256],
                         x=x2t[s]: e.tensor_tensor(o, p, x, Op.add),
                         deps=[ptok, wsafe])
            out_tok = mk.dma(mk.pool, mk.qpl,
                             dst[ob * 128:(ob + 1) * 128, t0q:t0q + 256],
                             outt, deps=[otok])
            wo_mm[s] = ptok
            wo_dve[s] = otok
            mk.wait(mk.dve, out_tok)
            mk.wait(mk.pe, otok)
        had_pe_last = mk.pe.tok()

    had_pe_last = None
    had_free = [None, None]
    wo_mm = [None, None]
    wo_dve = [None, None]

    def spsafe(mk=mk):
        mk.wait(mk.sp, (mk.qsp, mk.qsp.n))
        return mk.sp.inc(lambda e: e.nop())

    def plsafe(mk=mk):
        mk.wait(mk.pool, (mk.qpl, mk.qpl.n))
        return mk.pool.inc(lambda e: e.nop())
    dsl = 512
    rep_free = None
    c4 = {"spb": [None, None], "cons": [None, None], "ex": None, "dl": None,
          "uw": [None, None], "yw": [None, None], "ybw": [None, None],
          "sct": None, "tr": None}
    for tc in range(4):
        t0c = tc * dsl
        repdeps = [bc_tok] + (rep_free if rep_free else [])
        brtok = crtok = None
        prev_mm = [None, None]
        for n in range(2 * DS):
            sl = n % 2
            ldeps = repdeps + ([prev_mm[sl]] if prev_mm[sl] else [])
            bcl_tok = mk.dma(mk.sp, mk.qsp, bcrow[sl][0:1, :],
                             t["bcS"][n:n + 1, t0c:t0c + dsl],
                             deps=ldeps)
            bankb = pb[2 + sl]
            mtk = mk.mm(bankb[:, 0:dsl], g["oc_bf"][0:1, :],
                        bcrow[sl][0:1, :], True, True,
                        deps=[bcl_tok] + (rep_free if rep_free else []),
                        inc=True)
            prev_mm[sl] = mtk
            dstb = Brep if n < DS else Crep
            ntk = mk.ev(mk.dve, lambda e,
                        o=dstb[:, (n % DS) * dsl:(n % DS + 1) * dsl],
                        p=bankb[:, 0:dsl]: e.tensor_copy(o, p),
                        deps=[mtk])
            mk.wait(mk.pe, ntk)
            if n == DS - 1:
                brtok = ntk
            if n == 2 * DS - 1:
                crtok = ntk
        dtw = getattr(mk, "_c4_dtw", [None, None])
        mk._c4_dtw = dtw
        dttok = mk.dma(mk.sp, mk.qsp, dtbuf[tc % 2],
                       t["dtTS"][:, t0c:t0c + dsl],
                       deps=[dtT_tok] +
                       ([dtw[tc % 2]] if dtw[tc % 2] is not None else []))
        dttok = spsafe()
        def dbu_fn(e, o=dbu, d=du, b=Brep):
            last = None
            for n in range(DS):
                last = e.tensor_tensor(o[:, n * dsl:(n + 1) * dsl], d,
                                       b[:, n * dsl:(n + 1) * dsl],
                                       Op.mult)
            return last

        fr = {}

        def front(j, dttok=dttok, tc=tc, t0c=t0c):
            s = j % 2
            bank = pb[0 + s]
            dmdeps = [dttok, wdt_tok]
            if c4["spb"][s] is not None:
                dmdeps.append(c4["spb"][s])
            dmm = mk.mm(bank[:, 0:512], WdtT[:, j * 128:(j + 1) * 128],
                        dtbuf[tc % 2], True, True, deps=dmdeps, inc=True)
            if j == 31:
                dtw[tc % 2] = dmm

            def softp(e, p=bank[:, 0:512], t0_=tA, dl=delta2[s],
                      bias=g["dtb"][:, j:j + 1]):
                e.tensor_scalar(t0_, p, bias, 1.0, Op.add, Op.mult)
                return e.tensor_single_scalar(dl, t0_, 0.0, Op.max)
            spdeps = [dmm]
            if c4["ex"] is not None:
                spdeps.append(c4["ex"])
            if c4["cons"][s] is not None:
                spdeps.append(c4["cons"][s])
            sptok = mk.ev(mk.dve, softp, deps=spdeps)
            c4["spb"][s] = sptok

            def softp2(e, t0_=tA, ab=tB, ex=tC):
                e.activation(ab, t0_, AF.Abs)
                return e.activation(ex, ab, AF.Exp, scale=-1.0)
            s2deps = [sptok]
            if c4["dl"] is not None:
                s2deps.append(c4["dl"])
            extok = mk.ev(mk.act, softp2, deps=s2deps)
            c4["ex"] = extok

            def softp3(e, ex=tC, l=tB):
                return e.activation(l, ex, AF.Ln, bias=1.0)
            lntok = mk.ev(mk.act, softp3, deps=[extok])
            dltok = mk.ev(mk.dve, lambda e, dl=delta2[s], l=tB:
                          e.tensor_tensor(dl, dl, l, Op.add), deps=[lntok])
            c4["dl"] = dltok
            uw = [c4["uw"][s]] if c4["uw"][s] is not None else []
            utok = mk.dma(mk.sp, mk.qsp, ubuf[s],
                          t["uS"][j * 128:(j + 1) * 128, t0c:t0c + dsl],
                          deps=[(mk.qsp, u_tok[1])] + uw)
            sztok2 = mk.dma(mk.sp, mk.qsp, szbuf[s],
                            t["szS"][j * 128:(j + 1) * 128,
                                     t0c:t0c + dsl],
                            deps=[(mk.qsp, sz_tok[1])] + uw)
            usafe = spsafe()
            dutok = mk.ev(mk.dve, lambda e, o=du, dl=delta2[s], u=ubuf[s]:
                          e.tensor_tensor(o, dl, u, Op.mult),
                          deps=[dltok, usafe])
            fr[j] = (dltok, dutok, usafe)

        front(0)
        for di in range(32):
            s = di % 2
            dltok, dutok, usafe = fr.pop(di)
            datok = None
            dadeps = [dltok, g["negA_tok"]]
            if c4["sct"] is not None:
                dadeps.append(c4["sct"])
            for n in range(DS):
                datok = mk.ev(mk.act, lambda e,
                              o=dA[:, n * dsl:(n + 1) * dsl], i=delta2[s],
                              sc=g["negA"][:, di * DS + n:di * DS + n + 1]:
                              e.activation(o, i, AF.Exp, scale=sc),
                              deps=dadeps)
            c4["cons"][s] = datok
            if g.get("brep_const") and di == 0:
                mk.ev(mk.dve, lambda e, b=Brep: e.memset(b, 0.5),
                      deps=[brtok], inc=False)
            if g.get("du_const"):
                mk.ev(mk.dve, lambda e, d=du: e.memset(d, 0.01),
                      deps=[dutok], inc=False)
            dbudeps = [dutok, brtok]
            if c4["tr"] is not None:
                dbudeps.append(c4["tr"])
            dbutok = mk.ev(mk.dve, dbu_fn, deps=dbudeps)
            if di + 1 < 32:
                front(di + 1)
            if g.get("scan_const"):
                cm = g["scan_const"]
                def cfn(e, a=dA, b=dbu, cm=cm):
                    last = None
                    if "A" in cm:
                        last = e.memset(a, 0.5)
                    if "B" in cm:
                        last = e.memset(b, 0.01)
                    return last
                mk.ev(mk.dve, cfn, deps=[datok, dbutok])
            sctok = None
            for n in range(DS):
                sctok = mk.ev(mk.dve, lambda e,
                              o=Obuf[:, n * dsl:(n + 1) * dsl],
                              a=dA[:, n * dsl:(n + 1) * dsl],
                              b=dbu[:, n * dsl:(n + 1) * dsl],
                              i0=state[:, di * DS + n:di * DS + n + 1]:
                              e.tensor_tensor_scan(o, a, b, i0,
                                                   Op.mult, Op.add),
                              deps=[datok, dbutok, szero])
            c4["sct"] = sctok
            _dmp = []
            if g.get("dbg_scan") == (tc, di):
                _dmp.append(mk.dma(mk.pool, mk.qpl, t["dbgS"][:, 0:8192],
                                   dA, deps=[sctok]))
                _dmp.append(mk.dma(mk.pool, mk.qpl,
                                   t["dbgS"][:, 8192:16384], dbu,
                                   deps=[sctok]))
            sttok = mk.ev(mk.dve, lambda e,
                          o=state[:, di * DS:(di + 1) * DS].unsqueeze(2),
                          i=Obuf.rearrange("p (n t) -> p n t", n=DS)
                          [:, :, dsl - 1:dsl]:
                          e.tensor_copy(o, i), deps=[sctok])
            octok = mk.ev(mk.dve, lambda e, o=ocb, a=Obuf, b=Crep:
                          e.tensor_tensor(o, a, b, Op.mult),
                          deps=[sctok, crtok] + _dmp)
            ybank = pb[2 + s]
            ymdeps = [octok]
            if c4["ybw"][s] is not None:
                ymdeps.append(c4["ybw"][s])
            ymtk = None
            for n in range(DS):
                ymtk = mk.mm(ybank[:, 0:dsl], g["id_bf"],
                             ocb[:, n * dsl:(n + 1) * dsl],
                             n == 0, n == DS - 1,
                             deps=ymdeps if n == 0 else (),
                             inc=(n == DS - 1))
            c4["tr"] = ymtk

            def skipgate(e, yt=ytile, u=ubuf[s], Dv=g["Dv"][:, di:di + 1],
                         sz=szbuf[s], o=ygt[s], py=ybank[:, 0:dsl]):
                e.scalar_tensor_tensor(yt, u, Dv, py, Op.mult, Op.add)
                return e.tensor_tensor(o, yt, sz, Op.mult)
            mk.wait(mk.dve, (mk.qpl, mk.qpl.n))
            sgtok = mk.ev(mk.dve, skipgate, deps=[ymtk, usafe])
            c4["ybw"][s] = sgtok
            c4["uw"][s] = sgtok
            if g.get("dbg_scan") == (tc, di):
                mk.dma(mk.pool, mk.qpl, t["dbgS"][:, 16384:24576], Obuf,
                       deps=[sgtok])
                mk.dma(mk.pool, mk.qpl, t["dbgS"][:, 24576:25088],
                       delta2[s], deps=[sgtok])
                mk.dma(mk.pool, mk.qpl, t["dbgS"][:, 25088:33280], Brep,
                       deps=[sgtok])
            c4["yw"][s] = mk.dma(mk.pool, mk.qpl,
                                 t["yS"][di * 128:(di + 1) * 128,
                                         t0c:t0c + dsl], ygt[s],
                                 deps=[sgtok])
        y_toks.append((c4["yw"][0], c4["yw"][1]))
        rep_free = [(mk.dve, mk.dve.n), (mk.pool, mk.pool.n)]
        if tc >= 2 and not g.get("nohad"):
            emit_had(2 * (tc - 2))
            emit_had(2 * (tc - 2) + 1)
    if not g.get("nohad"):
        for q_ in (4, 5, 6, 7):
            emit_had(q_)
    else:
        g["x3_tok"] = y_toks[-1]
    g["x3_tok"] = out_tok
    ar.release()
    ar.release()


def _hadamard_bf16(n):
    h = np.array([[1.0]], np.float32)
    while h.shape[0] < n:
        h = np.block([[h, h], [h, -h]])
    return np.ascontiguousarray((h / math.sqrt(n)).astype(BFnp))


def kernel(**inputs):
    nc = build_program()
    ct = np.ascontiguousarray

    def f32(x):
        return ct(np.asarray(x))

    x = np.asarray(inputs["L0_attn_a_hidden_states"])
    common = {
        "Wq": f32(inputs["L0_attn_a_Wq"]),
        "Wk": f32(inputs["L0_attn_a_Wk"]),
        "Wv": f32(inputs["L0_attn_a_Wv"]),
        "Wg": f32(inputs["L0_attn_a_Wg"]),
        "Ws": f32(inputs["L0_attn_a_Ws"]),
        "Wo": f32(inputs["L0_attn_a_Wo"]),
        "gnw": f32(inputs["L0_attn_a_g_norm_weight"]).reshape(DV, 1),
        "W_inT": ct(np.asarray(inputs["L0_attn_b_W_in"]).T),
        "conv_w": ct(f32(inputs["L0_attn_b_conv_w"])
                     .reshape(32, 128, 4).transpose(1, 0, 2)
                     .reshape(128, 128)),
        "conv_b": ct(f32(inputs["L0_attn_b_conv_b"])
                     .reshape(32, 128).T),
        "W_xT": ct(np.asarray(inputs["L0_attn_b_W_x"]).T),
        "W_dtT": ct(np.asarray(inputs["L0_attn_b_W_dt"]).T),
        "dt_bias": ct(f32(inputs["L0_attn_b_dt_bias"])
                      .reshape(32, 128).T),
        "A_log": ct(f32(inputs["L0_attn_b_A_log"])
                    .reshape(32, 128, DS).transpose(1, 0, 2)
                    .reshape(128, 32 * DS)),
        "Dvec": ct(f32(inputs["L0_attn_b_D"]).reshape(32, 128).T),
        "W_outT": ct(np.asarray(inputs["L0_attn_b_W_out"]).T),
        "gupT": ct(np.asarray(inputs["L0_ffn_gate_up_packed"]).T),
        "gus": f32(inputs["L0_ffn_gate_up_scales"]).astype(
            np.float16, copy=False),
        "dnpT": ct(np.asarray(inputs["L0_ffn_down_packed"]).T),
        "dns": f32(inputs["L0_ffn_down_scales"]).astype(
            np.float16, copy=False),
        "Hmat": _hadamard_bf16(DI),
        "triu_bf": ct(np.triu(np.ones((128, 128), np.float32))
                      .astype(BFnp)),
        "ident_bf": ct(np.eye(128, dtype=np.float32).astype(BFnp)),
        "ident_f32": np.eye(128, dtype=np.float32),
        "ones_col_f32": np.ones((1, 128), np.float32),
        "ones_col_bf": np.ones((1, 128), np.float32).astype(BFnp),
        "ones128_f32": np.ones((128, 1), np.float32),
        "constf": np.tile(np.array([[1.0, 1e-5, 1e-6, 0.0]], np.float32),
                          (128, 1)),
        "rep8": np.repeat(np.eye(8, dtype=np.float32), 16,
                          axis=1).reshape(8, 128),
        "ones128_bf": np.ones((128, 1), np.float32).astype(BFnp),
    }
    in_maps = [dict(common, x0T=ct(np.asarray(x[b], np.float32).T))
               for b in range(8)]
    res = run_bass_kernel_spmd(nc, in_maps, core_ids=list(range(8)))
    out = np.stack([np.asarray(r["outT"]).T for r in res.results], axis=0)
    return np.ascontiguousarray(out)
